# version 02
# baseline (speedup 1.0000x reference)
import numpy as np
from contextlib import ExitStack
import concourse.bass as bass
import concourse.mybir as mybir
from concourse.bass_utils import run_bass_kernel_spmd

F32 = mybir.dt.float32
BF16 = mybir.dt.bfloat16
FP8 = mybir.dt.float8e4
AF = mybir.ActivationFunctionType
OP = mybir.AluOpType

H = 2048
DI = 4096
NST = 16
KC = 4
RR = 128
IFF = 6144
L = 16384
NC = 8
OWN = 2048
HALO = 128
W = OWN + HALO
NLT = W // 128
NOLT = OWN // 128
CH = 512
NCHK = OWN // CH
NDT = DI // 128
NHT = H // 128
NIT = IFF // 128
EPS = 1e-5
NCHUNKS = [(0, 512), (512, 512), (1024, 512), (1536, 512), (2048, 128)]


def build_program(mode, debug=()):
    st_mode = mode == "states"
    nc = bass.Bass("TRN2", target_bir_lowering=False, detect_race_conditions=False)

    def din(name, shape, dt=F32):
        return nc.dram_tensor(name, shape, dt, kind="ExternalInput").ap()

    xw = din("xw", [W, H])
    winT = din("winT", [H, 2 * DI])
    wxT = din("wxT", [DI, 160])
    wdtT = din("wdtT", [RR, DI])
    bdt_pk = din("bdt_pk", [128, NDT])
    bconv_pk = din("bconv_pk", [128, NDT])
    wconv_pk = din("wconv_pk", [128, NDT, KC])
    woutT = din("woutT", [DI, H])
    alog_pk = din("alog_pk", [128, NDT * NST])
    d_pk = din("d_pk", [128, NDT])
    gate_w = din("gate_w", [H, IFF])
    up_w = din("up_w", [H, IFF])
    down_w = din("down_w", [IFF, H])
    gsc_pk = din("gsc_pk", [128, NIT])
    usc_pk = din("usc_pk", [128, NIT])
    dsc_pk = din("dsc_pk", [128, NHT])
    prev_h = din("prev_h", [128, NC, NDT * NST])
    prev_s = din("prev_s", [128, NC, NDT])
    selm = din("selm", [32, 2 * NST * 128], BF16)

    if st_mode:
        stat_h = nc.dram_tensor("stat_h", [128, NDT * NST], F32,
                                kind="ExternalOutput").ap()
        stat_s = nc.dram_tensor("stat_s", [128, NDT], F32,
                                kind="ExternalOutput").ap()
    else:
        out = nc.dram_tensor("out", [OWN, H], F32, kind="ExternalOutput").ap()

    def hb(name, shape, dt):
        kind = "ExternalOutput" if name in debug else "Internal"
        return nc.dram_tensor(name, shape, dt, kind=kind).ap()

    winT_b = hb("winT_b", [H, 2 * DI], BF16)
    wxT_b = hb("wxT_b", [DI, 160], BF16)
    wdtT_b = hb("wdtT_b", [RR, DI], BF16)
    woutT_b = hb("woutT_b", [DI, H], BF16)
    gw8 = hb("gw8", [H, IFF], FP8)
    uw8 = hb("uw8", [H, IFF], FP8)
    dw8 = hb("dw8", [IFF, H], FP8)
    gwb = hb("gwb", [H, IFF], BF16)
    uwb = hb("uwb", [H, IFF], BF16)
    dwb = hb("dwb", [IFF, H], BF16)
    hs_sp = hb("hs_sp", [128, NDT, OWN], BF16)
    sg_sp = hb("sg_sp", [128, NDT, OWN], BF16)
    x1_sp = hb("x1_sp", [128, NOLT, H], F32)
    xn3T_sp = hb("xn3T_sp", [128, NHT, OWN], BF16)
    dbg_y = hb("dbg_y", [128, NDT, OWN], BF16) if "dbg_y" in debug else None
    dbg_fz = hb("dbg_fz", [128, NIT, OWN], BF16) if "dbg_fz" in debug else None
    dbg_fo = hb("dbg_fo", [128, NOLT, H], BF16) if "dbg_fo" in debug else None
    dbg_xnT = hb("dbg_xnT", [128, NHT, W], BF16) if "dbg_xnT" in debug else None
    dbg_st = hb("dbg_st", [128, 5, NLT], F32) if "dbg_st" in debug else None

    stk = ExitStack()
    with stk:
        sbt = lambda n, s, d: stk.enter_context(nc.sbuf_tensor(n, s, d))
        pst = lambda n, s: stk.enter_context(nc.psum_tensor(n, s, F32))

        wx_sb = sbt("wx_sb", [128, NDT, 160], BF16)
        wdt_sb = sbt("wdt_sb", [128, DI], BF16)
        a_sb = sbt("a_sb", [128, NDT * NST], F32)
        bdt_sb = sbt("bdt_sb", [128, NDT], F32)
        bcv_sb = sbt("bcv_sb", [128, NDT], F32)
        wcv_sb = sbt("wcv_sb", [128, NDT, KC], F32)
        d_sb = sbt("d_sb", [128, NDT], F32)
        gsc_sb = sbt("gsc_sb", [128, NIT], F32)
        usc_sb = sbt("usc_sb", [128, NIT], F32)
        dsc_sb = sbt("dsc_sb", [128, NHT], F32)
        hst_sb = sbt("hst_sb", [128, NDT * NST], F32)
        stot_sb = sbt("stot_sb", [128, NDT], F32)
        sq_sb = sbt("sq_sb", [128, H], F32)
        st1 = sbt("st1", [128, NLT], F32)
        st2 = sbt("st2", [128, NLT], F32)
        st3 = sbt("st3", [128, NLT], F32)
        st4 = sbt("st4", [128, NLT], F32)
        stt = sbt("stt", [128, NLT], F32)
        xr_sb = sbt("xr_sb", [128, H], F32)

        stkB = ExitStack()
        sbtB = lambda n, s, d: stkB.enter_context(nc.sbuf_tensor(n, s, d))
        apk_sb = sbtB("apk_sb", [128, NDT * NST], F32)
        pvh_sb = sbtB("pvh_sb", [128, NDT * NST], F32)
        pvs_sb = sbtB("pvs_sb", [128, NDT], F32)
        ew_sb = sbtB("ew_sb", [128, NDT * NST], F32)
        xt_sb = [sbtB(f"xt{i}", [128, H], F32) for i in range(2)]
        xn_sb = [sbtB(f"xn{i}", [128, H], BF16) for i in range(2)]
        xnT_sb = sbtB("xnT_sb", [128, NHT, W], BF16)
        win_sb = [sbtB(f"win{i}", [128, NHT, 128], BF16) for i in range(2)]
        prj_sb = sbtB("prj_sb", [128, W], F32)
        cacc_sb = sbtB("cacc_sb", [128, OWN], F32)
        hso_sb = sbtB("hso_sb", [128, OWN], BF16)

        pA = pst("pA", [128, 512])
        pB = pst("pB", [128, 512])
        pS1 = pst("pS1", [128, 512])
        pS2 = pst("pS2", [32, 512])
        pD1 = pst("pD1", [128, 512])
        pD2 = pst("pD2", [128, 512])
        pW1 = pst("pW1", [128, 512])
        pW2 = pst("pW2", [128, 512])

        names = ["sa", "sv", "sp", "st", "xb", "od", "spl", "xrd", "q3d",
                 "xd0", "xd1", "wd0", "wd1", "hd0", "hd1", "h3d0", "h3d1",
                 "wod0", "wod1", "gud0", "gud1", "dwd0", "dwd1",
                 "gds", "gdw", "gdo", "gdf", "gdp"]
        sh = {n: stk.enter_context(nc.semaphore(n)) for n in names}
        sv = {n: 0 for n in names}

        def inc(inst, n, k=1):
            inst.then_inc(sh[n], k)
            sv[n] += k
            return sv[n]

        blk = stk.enter_context(nc.Block())

        @blk.gpsimd
        def _(g):
            def dma(o, i):
                g.dma_start(out=o, in_=i).then_inc(sh["gds"], 16)
                sv["gds"] += 16
            dma(apk_sb[:, :], alog_pk[:, :])
            dma(bdt_sb[:, :], bdt_pk[:, :])
            dma(bcv_sb[:, :], bconv_pk[:, :])
            dma(wcv_sb[:, :, :], wconv_pk[:, :, :])
            dma(d_sb[:, :], d_pk[:, :])
            dma(gsc_sb[:, :], gsc_pk[:, :])
            dma(usc_sb[:, :], usc_pk[:, :])
            dma(dsc_sb[:, :], dsc_pk[:, :])
        gd_small = sv["gds"]

        @blk.gpsimd
        def _(g):
            def dma(o, i):
                g.dma_start(out=o, in_=i).then_inc(sh["gdw"], 16)
                sv["gdw"] += 16
            dma(winT_b[:, :], winT[:, :])
            dma(wxT_b[:, :], wxT[:, :])
            dma(wdtT_b[:, :], wdtT[:, :])
            g.wait_ge(sh["gdw"], sv["gdw"])
            dma(wx_sb[:, :, :],
                wxT_b.rearrange("(dt p) r -> p dt r", p=128)[:, :, :])
            dma(wdt_sb[:, :], wdtT_b[:, :])
        gd_win = 48
        gd_mamba = sv["gdw"]

        if not st_mode:
            @blk.gpsimd
            def _(g):
                g.dma_start(out=woutT_b[:, :], in_=woutT[:, :]
                            ).then_inc(sh["gdo"], 16)
                sv["gdo"] += 16

                def dma(o, i):
                    g.dma_start(out=o, in_=i).then_inc(sh["gdf"], 16)
                    sv["gdf"] += 16
                dma(gw8[:, :], gate_w[:, :])
                dma(uw8[:, :], up_w[:, :])
                dma(dw8[:, :], down_w[:, :])
                g.wait_ge(sh["gdf"], sv["gdf"])
                dma(gwb[:, :], gw8[:, :])
                dma(uwb[:, :], uw8[:, :])
                dma(dwb[:, :], dw8[:, :])
        gd_wout = sv["gdo"]
        gd_all = sv["gdf"]

        @blk.scalar
        def _(sc):
            sc.wait_ge(sh["gds"], gd_small)
            inc(sc.activation(out=a_sb[:, :], in_=apk_sb[:, :], func=AF.Exp), "sa")
        sa_exp = sv["sa"]

        @blk.vector
        def _(v):
            v.wait_ge(sh["sa"], sa_exp)
            v.tensor_scalar_mul(out=a_sb[:, :], in0=a_sb[:, :], scalar1=-1.0)
            v.memset(hst_sb[:, :], 0.0)
            inc(v.memset(stot_sb[:, :], 0.0), "sv")
        sv_a = sv["sv"]

        if not st_mode:
            a3 = a_sb[:, :].rearrange("p (dt n) -> p dt n", n=NST)
            e3 = ew_sb[:, :].rearrange("p (dt n) -> p dt n", n=NST)
            for j in range(NC - 1):
                @blk.gpsimd
                def _(g, j=j):
                    if j >= 1:
                        g.wait_ge(sh["sv"], sv["sv"])
                    g.dma_start(out=pvh_sb[:, :], in_=prev_h[:, j, :]
                                ).then_inc(sh["gdp"], 16)
                    sv["gdp"] += 16
                    g.dma_start(out=pvs_sb[:, :], in_=prev_s[:, j, :]
                                ).then_inc(sh["gdp"], 16)
                    sv["gdp"] += 16
                pvc = sv["gdp"]

                @blk.vector
                def _(v, j=j, pvc=pvc):
                    if j == 0:
                        v.wait_ge(sh["sv"], sv_a)
                    v.wait_ge(sh["gdp"], pvc)
                    sbc = pvs_sb[:, :].unsqueeze(2).broadcast_to([128, NDT, NST])
                    inc(v.tensor_mul(out=e3, in0=a3, in1=sbc), "sv")
                vme = sv["sv"]

                @blk.scalar
                def _(sc, vme=vme):
                    sc.wait_ge(sh["sv"], vme)
                    inc(sc.activation(out=ew_sb[:, :], in_=ew_sb[:, :], func=AF.Exp), "sa")
                sae = sv["sa"]

                @blk.vector
                def _(v, j=j, sae=sae):
                    v.wait_ge(sh["sa"], sae)
                    v.tensor_mul(out=hst_sb[:, :], in0=hst_sb[:, :], in1=ew_sb[:, :])
                    inc(v.tensor_add(out=hst_sb[:, :], in0=hst_sb[:, :],
                                 in1=pvh_sb[:, :]), "sv")
        sv_p0 = sv["sv"]

        saA = sv["sa"]
        svA = sv["sv"]
        for lt in range(NLT):
            b = lt % 2
            xdn = f"xd{b}"

            @blk.sync
            def _(sy, lt=lt, b=b, xdn=xdn):
                if lt >= 2:
                    sy.wait_ge(sh["sa"], saA + lt - 1)
                    sy.wait_ge(sh["sv"], svA + lt - 1)
                sy.dma_start(out=xt_sb[b][:, :],
                             in_=xw[lt * 128:(lt + 1) * 128, :]
                             ).then_inc(sh[xdn], 16)
                sv[xdn] += 16
            dcount = sv[xdn]

            @blk.scalar
            def _(sc, lt=lt, b=b, dcount=dcount, xdn=xdn):
                sc.wait_ge(sh[xdn], dcount)
                sc.wait_ge(sh["sv"], sv["sv"])
                inc(sc.activation(out=sq_sb[:, :], in_=xt_sb[b][:, :],
                                  func=AF.Square), "sa")
            sqdone = sv["sa"]

            @blk.vector
            def _(v, lt=lt, b=b, dcount=dcount, xdn=xdn, sqdone=sqdone):
                v.wait_ge(sh[xdn], dcount)
                v.reduce_sum(out=st1[:, lt:lt + 1], in_=xt_sb[b][:, :],
                             axis=mybir.AxisListType.X)
                v.wait_ge(sh["sa"], sqdone)
                inc(v.reduce_sum(out=st2[:, lt:lt + 1], in_=sq_sb[:, :],
                             axis=mybir.AxisListType.X), "sv")

        @blk.vector
        def _(v):
            v.wait_ge(sh["sa"], sv["sa"])
            inc(v.tensor_scalar_mul(out=st4[:, :], in0=st1[:, :], scalar1=1.0 / H), "sv")
            v.wait_ge(sh["sv"], sv["sv"])
            inc(v.tensor_mul(out=stt[:, :], in0=st4[:, :], in1=st4[:, :]), "sv")
            inc(v.tensor_scalar_mul(out=st3[:, :], in0=st2[:, :], scalar1=1.0 / H), "sv")
            v.wait_ge(sh["sv"], sv["sv"])
            inc(v.tensor_sub(out=st3[:, :], in0=st3[:, :], in1=stt[:, :]), "sv")
        vstat = sv["sv"]

        @blk.scalar
        def _(sc):
            sc.wait_ge(sh["sv"], vstat)
            inc(sc.activation(out=st3[:, :], in_=st3[:, :], func=AF.Ln,
                              bias=np.float32(EPS)), "sa")
            sc.wait_ge(sh["sa"], sv["sa"])
            inc(sc.activation(out=st3[:, :], in_=st3[:, :], func=AF.Exp,
                              scale=np.float32(-0.5)), "sa")
        astat = sv["sa"]

        @blk.vector
        def _(v):
            v.wait_ge(sh["sa"], astat)
            inc(v.tensor_mul(out=st4[:, :], in0=st4[:, :], in1=st3[:, :]), "sv")
            v.wait_ge(sh["sv"], sv["sv"])
            inc(v.tensor_scalar_mul(out=st4[:, :], in0=st4[:, :], scalar1=-1.0), "sv")
        vstat2 = sv["sv"]

        saC = sv["sa"]
        xbC = sv["xb"]
        for lt in range(NLT):
            b = lt % 2
            xdn = f"xd{b}"

            @blk.sync
            def _(sy, lt=lt, b=b, xdn=xdn):
                if lt < 2:
                    sy.wait_ge(sh["sa"], astat)
                    sy.wait_ge(sh["sv"], vstat2)
                else:
                    sy.wait_ge(sh["sa"], saC + lt - 1)
                sy.dma_start(out=xt_sb[b][:, :],
                             in_=xw[lt * 128:(lt + 1) * 128, :]
                             ).then_inc(sh[xdn], 16)
                sv[xdn] += 16
            dcount = sv[xdn]

            @blk.scalar
            def _(sc, lt=lt, b=b, dcount=dcount, xdn=xdn):
                sc.wait_ge(sh[xdn], dcount)
                if lt == 0:
                    sc.wait_ge(sh["sv"], vstat2)
                if lt >= 2:
                    sc.wait_ge(sh["xb"], xbC + (lt - 1) * NHT * 16)
                inc(sc.activation(out=xn_sb[b][:, :], in_=xt_sb[b][:, :], func=AF.Identity,
                              scale=st3[:, lt:lt + 1], bias=st4[:, lt:lt + 1]), "sa")
            acount = sv["sa"]

            @blk.sync
            def _(sy, lt=lt, b=b, acount=acount):
                sy.wait_ge(sh["sa"], acount)
                for ht in range(NHT):
                    sy.dma_start_transpose(
                        xnT_sb[:, ht, lt * 128:(lt + 1) * 128],
                        xn_sb[b][:, ht * 128:(ht + 1) * 128],
                    ).then_inc(sh["xb"], 16)
                    sv["xb"] += 16
        p1_xb = sv["xb"]

        if dbg_xnT is not None:
            @blk.sync
            def _(sy):
                sy.wait_ge(sh["xb"], p1_xb)
                sy.dma_start(out=dbg_xnT[:, :, :], in_=xnT_sb[:, :, :]
                             ).then_inc(sh["spl"], 16)
                sv["spl"] += 16
        if dbg_st is not None:
            @blk.sync
            def _(sy):
                sy.wait_ge(sh["sv"], sv["sv"])
                sy.wait_ge(sh["sa"], sv["sa"])
                for i, t_ in enumerate([st1, st2, st3, st4, stt]):
                    sy.dma_start(out=dbg_st[:, i, :], in_=t_[:, :]
                                 ).then_inc(sh["spl"], 16)
                    sv["spl"] += 16

        NE = NDT if st_mode else 2 * NDT
        stP2 = sv["st"]
        for et in range(NE):
            b = et % 2

            wdn = f"wd{b}"

            @blk.sync
            def _(sy, et=et, b=b, wdn=wdn):
                if et == 0:
                    sy.wait_ge(sh["gdw"], gd_win)
                if et >= 2:
                    sy.wait_ge(sh["st"], stP2 + 5 * (et - 1))
                sy.dma_start(
                    out=win_sb[b][:, :, :],
                    in_=winT_b.rearrange("(ht p) e -> p ht e", p=128)[
                        :, :, et * 128:(et + 1) * 128],
                ).then_inc(sh[wdn], 16)
                sv[wdn] += 16
            wcount = sv[wdn]
            sa_tile = sv["sa"]

            @blk.tensor
            def _(t, et=et, b=b, wcount=wcount, sa_tile=sa_tile, wdn=wdn):
                t.wait_ge(sh[wdn], wcount)
                if et == 0:
                    t.wait_ge(sh["xb"], p1_xb)
                for ci, (c0, cw) in enumerate(NCHUNKS):
                    bank = pA if ci % 2 == 0 else pB
                    if ci < 2:
                        t.wait_ge(sh["sa"], sa_tile)
                    else:
                        t.wait_ge(sh["sa"], sa_tile + ci - 1)
                    for ht in range(NHT):
                        mm = t.matmul(bank[:, 0:cw], win_sb[b][:, ht, :],
                                      xnT_sb[:, ht, c0:c0 + cw],
                                      start=(ht == 0), stop=(ht == NHT - 1))
                    inc(mm, "st")
            tbase = sv["st"] - 5

            if et < NDT:
                for ci, (c0, cw) in enumerate(NCHUNKS):
                    @blk.scalar
                    def _(sc, ci=ci, c0=c0, cw=cw, tbase=tbase):
                        sc.wait_ge(sh["st"], tbase + ci + 1)
                        if ci == 0:
                            sc.wait_ge(sh["sv"], sv["sv"])
                        bank = pA if ci % 2 == 0 else pB
                        inc(sc.activation(out=prj_sb[:, c0:c0 + cw], in_=bank[:, 0:cw],
                                      func=AF.Copy), "sa")
                ecount = sv["sa"]

                @blk.vector
                def _(v, et=et, ecount=ecount):
                    v.wait_ge(sh["sa"], ecount)
                    v.tensor_scalar_mul(out=cacc_sb[:, :],
                                        in0=prj_sb[:, HALO - 3:HALO - 3 + OWN],
                                        scalar1=wcv_sb[:, et, 0:1])
                    for k in range(1, KC):
                        inc(v.scalar_tensor_tensor(
                            out=cacc_sb[:, :],
                            in0=prj_sb[:, HALO - 3 + k:HALO - 3 + k + OWN],
                            scalar=wcv_sb[:, et, k:k + 1],
                            in1=cacc_sb[:, :], op0=OP.mult, op1=OP.add), "sv")
                ccount = sv["sv"]

                @blk.scalar
                def _(sc, et=et, ccount=ccount):
                    sc.wait_ge(sh["sv"], ccount)
                    sc.wait_ge(sh["spl"], sv["spl"])
                    inc(sc.activation(out=hso_sb[:, :], in_=cacc_sb[:, :], func=AF.Silu,
                                  bias=bcv_sb[:, et:et + 1]), "sa")
                hcount = sv["sa"]

                @blk.sync
                def _(sy, et=et, hcount=hcount):
                    sy.wait_ge(sh["sa"], hcount)
                    sy.dma_start(out=hs_sp[:, et, :], in_=hso_sb[:, :]
                                 ).then_inc(sh["spl"], 16)
                    sv["spl"] += 16
            else:
                dte = et - NDT
                for ci, (c0, cw) in enumerate(NCHUNKS):
                    @blk.scalar
                    def _(sc, ci=ci, c0=c0, cw=cw, tbase=tbase):
                        sc.wait_ge(sh["st"], tbase + ci + 1)
                        if ci == 0:
                            sc.wait_ge(sh["sv"], sv["sv"])
                        bank = pA if ci % 2 == 0 else pB
                        inc(sc.activation(out=prj_sb[:, c0:c0 + cw], in_=bank[:, 0:cw],
                                      func=AF.Silu), "sa")
                ecount = sv["sa"]

                @blk.vector
                def _(v, ecount=ecount):
                    v.wait_ge(sh["sa"], ecount)
                    v.wait_ge(sh["spl"], sv["spl"])
                    inc(v.tensor_copy(out=hso_sb[:, :], in_=prj_sb[:, HALO:]), "sv")
                gcount = sv["sv"]

                @blk.sync
                def _(sy, dte=dte, gcount=gcount):
                    sy.wait_ge(sh["sv"], gcount)
                    sy.dma_start(out=sg_sp[:, dte, :], in_=hso_sb[:, :]
                                 ).then_inc(sh["spl"], 16)
                    sv["spl"] += 16

        p2_spl = sv["spl"]
        p2_st = sv["st"]

        stkB.close()
        stkP3 = ExitStack()
        sbt3 = lambda n, s, d: stkP3.enter_context(nc.sbuf_tensor(n, s, d))
        sel_sb = sbt3("sel_sb", [32, 2 * NST * 128], BF16)
        if not st_mode:
            y_sb = sbt3("y_sb", [128, NDT, CH], BF16)

        @blk.sync
        def _(sy):
            sy.wait_ge(sh["st"], p2_st)
            sy.wait_ge(sh["spl"], p2_spl)
            sy.dma_start(out=sel_sb[:, :], in_=selm[:, :]).then_inc(sh["spl"], 16)
            sv["spl"] += 16
        sel_spl = sv["spl"]

        last_uc_holder = [None]
        for c in range(NCHK):
            c0 = c * CH
            stkC = ExitStack()
            sbtC = lambda n, s, d: stkC.enter_context(
                nc.sbuf_tensor(f"{n}_c{c}", s, d))
            hs_ch = [sbtC(f"hsch{i}", [128, CH], BF16) for i in range(2)]
            dtin_sb = sbtC("dtin_sb", [128, CH], BF16)
            bc_sb = sbtC("bc_sb", [32, CH], BF16)
            bbc_sb = sbtC("bbc_sb", [128, NST, CH], BF16)
            cbc_sb = sbtC("cbc_sb", [128, NST, CH], BF16)
            dt_sb = [sbtC(f"dt{i}", [128, CH], F32) for i in range(2)]
            sp1_sb = sbtC("sp1_sb", [128, CH], F32)
            hs3_sb = [sbtC(f"hs3{i}", [128, CH], BF16) for i in range(2)]
            sg3_sb = [sbtC(f"sg3{i}", [128, CH], BF16) for i in range(2)]
            u_sb = sbtC("u_sb", [128, CH], BF16)
            pt_sb = sbtC("pt_sb", [128, CH], BF16)
            da_sb = sbtC("da_sb", [128, 4, CH], F32)
            dbu_sb = sbtC("dbu_sb", [128, NST, CH], BF16)
            h_sb = sbtC("h_sb", [128, NST, CH], BF16)
            pr_sb = sbtC("pr_sb", [128, 8, CH], BF16)
            pr2_sb = sbtC("pr2_sb", [128, 2, CH], F32)
            stS = sv["st"]
            for dt_i in range(NDT):
                b = dt_i % 2

                hdn = f"hd{b}"

                @blk.sync
                def _(sy, c=c, dt_i=dt_i, b=b, c0=c0, stS=stS, hdn=hdn):
                    if dt_i == 0:
                        sy.wait_ge(sh["spl"], p2_spl)
                        sy.wait_ge(sh["gdw"], gd_mamba)
                        sy.wait_ge(sh["st"], sv["st"])
                        sy.wait_ge(sh["sa"], sv["sa"])
                        sy.wait_ge(sh["sv"], sv["sv"])
                        sy.wait_ge(sh["xb"], sv["xb"])
                    if dt_i >= 2:
                        sy.wait_ge(sh["st"], stS + dt_i - 1)
                    sy.dma_start(out=hs_ch[b][:, :], in_=hs_sp[:, dt_i, c0:c0 + CH]
                                 ).then_inc(sh[hdn], 16)
                    sv[hdn] += 16
                hcount = sv[hdn]

                @blk.tensor
                def _(t, dt_i=dt_i, b=b, hcount=hcount, hdn=hdn):
                    t.wait_ge(sh[hdn], hcount)
                    if dt_i == 0:
                        t.wait_ge(sh["sa"], sv["sa"])
                    t.matmul(pS1[:, :], wx_sb[:, dt_i, 0:128], hs_ch[b][:, :],
                             start=(dt_i == 0), stop=(dt_i == NDT - 1))
                    inc(t.matmul(pS2[:, :], wx_sb[:, dt_i, 128:160], hs_ch[b][:, :],
                             start=(dt_i == 0), stop=(dt_i == NDT - 1)), "st")
            scount = sv["st"]

            @blk.scalar
            def _(sc, scount=scount):
                sc.wait_ge(sh["st"], scount)
                sc.activation(out=dtin_sb[:, :], in_=pS1[:, :], func=AF.Copy)
                inc(sc.activation(out=bc_sb[:, :], in_=pS2[:, :], func=AF.Copy), "sa")
            bccount = sv["sa"]

            stB = sv["st"]
            saB = sv["sa"]

            @blk.tensor
            def _(t, bccount=bccount, stB=stB, saB=saB):
                t.wait_ge(sh["sa"], bccount)
                t.wait_ge(sh["spl"], sel_spl)
                t.wait_ge(sh["sp"], sv["sp"])
                for n in range(2 * NST):
                    bank = pD1 if n % 2 == 0 else pD2
                    if n >= 2:
                        t.wait_ge(sh["sa"], saB + n - 1)
                    inc(t.matmul(bank[:, :], sel_sb[:, n * 128:(n + 1) * 128],
                             bc_sb[:, :], start=True, stop=True), "st")

            @blk.scalar
            def _(sc, stB=stB):
                for n in range(2 * NST):
                    bank = pD1 if n % 2 == 0 else pD2
                    sc.wait_ge(sh["st"], stB + n + 1)
                    dst = bbc_sb if n < NST else cbc_sb
                    inc(sc.activation(out=dst[:, n % NST, :], in_=bank[:, :],
                                  func=AF.Copy), "sa")
            bcastdone = sv["sa"]

            for dt_i in range(NDT):
                bb = dt_i % 2

                h3n = f"h3d{bb}"

                @blk.sync
                def _(sy, dt_i=dt_i, bb=bb, c0=c0, h3n=h3n):
                    sy.wait_ge(sh["sv"], sv["sv"])
                    sy.wait_ge(sh["sp"], sv["sp"])
                    sy.dma_start(out=hs3_sb[bb][:, :], in_=hs_sp[:, dt_i, c0:c0 + CH]
                                 ).then_inc(sh[h3n], 16)
                    sv[h3n] += 16
                    if not st_mode:
                        sy.dma_start(out=sg3_sb[bb][:, :],
                                     in_=sg_sp[:, dt_i, c0:c0 + CH]
                                     ).then_inc(sh[h3n], 16)
                        sv[h3n] += 16
                h3count = sv[h3n]

                @blk.tensor
                def _(t, dt_i=dt_i):
                    t.wait_ge(sh["sa"], sv["sa"])
                    bank = pW1 if dt_i % 2 == 0 else pW2
                    inc(t.matmul(bank[:, :], wdt_sb[:, dt_i * 128:(dt_i + 1) * 128],
                             dtin_sb[:, :], start=True, stop=True), "st")
                dtc = sv["st"]

                @blk.scalar
                def _(sc, dt_i=dt_i, bb=bb, dtc=dtc):
                    sc.wait_ge(sh["st"], dtc)
                    if dt_i == 0:
                        sc.wait_ge(sh["sv"], sv_p0)
                    bank = pW1 if dt_i % 2 == 0 else pW2
                    inc(sc.activation(out=sp1_sb[:, :], in_=bank[:, :], func=AF.Abs,
                                  bias=bdt_sb[:, dt_i:dt_i + 1]), "sa")
                    sc.wait_ge(sh["sa"], sv["sa"])
                    inc(sc.activation(out=sp1_sb[:, :], in_=sp1_sb[:, :],
                                  func=AF.Exp, scale=np.float32(-1.0)), "sa")
                    sc.wait_ge(sh["sa"], sv["sa"])
                    inc(sc.activation(out=sp1_sb[:, :], in_=sp1_sb[:, :],
                                  func=AF.Ln, bias=np.float32(1.0)), "sa")
                    sc.wait_ge(sh["sa"], sv["sa"])
                    inc(sc.activation(out=dt_sb[bb][:, :], in_=bank[:, :], func=AF.Relu,
                                  bias=bdt_sb[:, dt_i:dt_i + 1]), "sa")
                spc = sv["sa"]

                @blk.vector
                def _(v, dt_i=dt_i, bb=bb, spc=spc, h3count=h3count, h3n=h3n):
                    v.wait_ge(sh["sa"], spc)
                    v.tensor_add(out=dt_sb[bb][:, :], in0=dt_sb[bb][:, :],
                                 in1=sp1_sb[:, :])
                    v.wait_ge(sh[h3n], h3count)
                    v.wait_ge(sh["sp"], sv["sp"])
                    inc(v.tensor_mul(out=u_sb[:, :], in0=dt_sb[bb][:, :],
                                 in1=hs3_sb[bb][:, :]), "sv")
                    if st_mode:
                        inc(v.reduce_sum(out=st1[:, 0:1], in_=dt_sb[bb][:, :],
                                     axis=mybir.AxisListType.X), "sv")
                        v.wait_ge(sh["sv"], sv["sv"])
                        v.tensor_add(out=stot_sb[:, dt_i:dt_i + 1],
                                     in0=stot_sb[:, dt_i:dt_i + 1],
                                     in1=st1[:, 0:1])
                uc = sv["sv"]

                pool_base = sv["sp"]
                act_base = sv["sa"]
                dve_base = uc

                @blk.gpsimd
                def _(g, dt_i=dt_i, uc=uc):
                    g.wait_ge(sh["sv"], uc)
                    for n in range(NST):
                        inc(g.tensor_tensor(out=dbu_sb[:, n, :], in0=u_sb[:, :],
                                        in1=bbc_sb[:, n, :], op=OP.mult), "sp")

                @blk.scalar
                def _(sc, dt_i=dt_i, bb=bb, dve_base=dve_base,
                      last_uc=last_uc_holder[0]):
                    for n in range(NST):
                        if n == 0:
                            sc.wait_ge(sh["sv"], dve_base)
                        if n >= 4:
                            sc.wait_ge(sh["sv"], dve_base + n - 3)
                        elif last_uc is not None:
                            sc.wait_ge(sh["sv"], last_uc + n + 13)
                        col = dt_i * NST + n
                        inc(sc.activation(out=da_sb[:, n % 4, :], in_=dt_sb[bb][:, :],
                                      func=AF.Exp, scale=a_sb[:, col:col + 1]), "sa")
                last_uc_holder[0] = uc

                @blk.vector
                def _(v, dt_i=dt_i, pool_base=pool_base, act_base=act_base):
                    for n in range(NST):
                        v.wait_ge(sh["sa"], act_base + n + 1)
                        v.wait_ge(sh["sp"], pool_base + n + 1)
                        col = dt_i * NST + n
                        inc(v.tensor_tensor_scan(
                            out=h_sb[:, n, :], data0=da_sb[:, n % 4, :],
                            data1=dbu_sb[:, n, :],
                            initial=hst_sb[:, col:col + 1],
                            op0=OP.mult, op1=OP.add), "sv")
                    v.wait_ge(sh["sv"], sv["sv"])
                    inc(v.tensor_copy(
                        out=hst_sb[:, dt_i * NST:(dt_i + 1) * NST],
                        in_=h_sb[:, :, CH - 1:CH].squeeze(2)), "sv")
                scdone = sv["sv"]

                if not st_mode:
                    @blk.gpsimd
                    def _(g, scdone=scdone):
                        base = scdone - NST - 1
                        for k in range(8):
                            g.wait_ge(sh["sv"], base + 2 * k + 2)
                            g.tensor_tensor(out=pr_sb[:, k, :],
                                            in0=h_sb[:, 2 * k, :],
                                            in1=cbc_sb[:, 2 * k, :], op=OP.mult)
                            g.tensor_tensor(out=pt_sb[:, :],
                                            in0=h_sb[:, 2 * k + 1, :],
                                            in1=cbc_sb[:, 2 * k + 1, :],
                                            op=OP.mult)
                            inc(g.tensor_tensor(out=pr_sb[:, k, :],
                                            in0=pr_sb[:, k, :], in1=pt_sb[:, :],
                                            op=OP.add), "sp")
                    prc = sv["sp"]

                    @blk.vector
                    def _(v, dt_i=dt_i, bb=bb, prc=prc):
                        v.wait_ge(sh["sp"], prc)
                        for k in range(4):
                            v.tensor_add(out=pr_sb[:, k, :], in0=pr_sb[:, 2 * k, :],
                                         in1=pr_sb[:, 2 * k + 1, :])
                        v.tensor_add(out=pr2_sb[:, 0, :], in0=pr_sb[:, 0, :],
                                     in1=pr_sb[:, 1, :])
                        v.tensor_add(out=pr2_sb[:, 1, :], in0=pr_sb[:, 2, :],
                                     in1=pr_sb[:, 3, :])
                        v.tensor_add(out=pr2_sb[:, 0, :], in0=pr2_sb[:, 0, :],
                                     in1=pr2_sb[:, 1, :])
                        v.scalar_tensor_tensor(out=pr2_sb[:, 0, :],
                                               in0=hs3_sb[bb][:, :],
                                               scalar=d_sb[:, dt_i:dt_i + 1],
                                               in1=pr2_sb[:, 0, :],
                                               op0=OP.mult, op1=OP.add)
                        inc(v.tensor_mul(out=y_sb[:, dt_i, :], in0=pr2_sb[:, 0, :],
                                     in1=sg3_sb[bb][:, :]), "sv")

            if st_mode:
                stkC.close()
                continue

            ydone = sv["sv"]

            stkC.close()
            stkC2 = ExitStack()
            sbtC2 = lambda n, s, d: stkC2.enter_context(
                nc.sbuf_tensor(f"{n}_c{c}", s, d))
            wo_sb = [sbtC2(f"wo{i}", [128, NDT, 128], BF16) for i in range(2)]
            moT_sb = [sbtC2(f"moT{i}", [128, CH], BF16) for i in range(2)]
            mo_sb = sbtC2("mo_sb", [128, 4, H], BF16)
            x1_sb = sbtC2("x1_sb", [128, H], F32)
            xn3_sb = sbtC2("xn3_sb", [128, H], BF16)
            x3T_sb = sbtC2("x3T_sb", [128, NHT, CH], BF16)

            stW = sv["st"]
            saW = sv["sa"]
            for ht in range(NHT):
                b = ht % 2

                wdn = f"wod{b}"

                @blk.sync
                def _(sy, ht=ht, b=b, stW=stW, ydone=ydone, wdn=wdn):
                    if ht == 0:
                        sy.wait_ge(sh["sv"], ydone)
                        sy.wait_ge(sh["sp"], sv["sp"])
                        sy.wait_ge(sh["sa"], sv["sa"])
                        sy.wait_ge(sh["gdo"], gd_wout)
                    if ht >= 2:
                        sy.wait_ge(sh["st"], stW + ht - 1)
                    sy.dma_start(
                        out=wo_sb[b][:, :, :],
                        in_=woutT_b.rearrange("(dt p) h -> p dt h", p=128)[
                            :, :, ht * 128:(ht + 1) * 128],
                    ).then_inc(sh[wdn], 16)
                    sv[wdn] += 16
                wc = sv[wdn]

                @blk.tensor
                def _(t, ht=ht, b=b, wc=wc, ydone=ydone, saW=saW, wdn=wdn):
                    t.wait_ge(sh[wdn], wc)
                    if ht == 0:
                        t.wait_ge(sh["sv"], ydone)
                        t.wait_ge(sh["sa"], saW)
                    if ht >= 2:
                        t.wait_ge(sh["sa"], saW + ht - 1)
                    bank = pW1 if ht % 2 == 0 else pW2
                    for dt_i in range(NDT):
                        mm = t.matmul(bank[:, :], wo_sb[b][:, dt_i, :],
                                      y_sb[:, dt_i, :],
                                      start=(dt_i == 0), stop=(dt_i == NDT - 1))
                    inc(mm, "st")
                mmc = sv["st"]

                @blk.scalar
                def _(sc, ht=ht, mmc=mmc):
                    sc.wait_ge(sh["st"], mmc)
                    if ht < 2:
                        sc.wait_ge(sh["xb"], sv["xb"])
                    bank = pW1 if ht % 2 == 0 else pW2
                    inc(sc.activation(out=moT_sb[ht % 2][:, :], in_=bank[:, :],
                                  func=AF.Copy), "sa")
                ec = sv["sa"]

                @blk.sync
                def _(sy, ht=ht, ec=ec):
                    sy.wait_ge(sh["sa"], ec)
                    for lb in range(4):
                        sy.dma_start_transpose(
                            mo_sb[:, lb, ht * 128:(ht + 1) * 128],
                            moT_sb[ht % 2][:, lb * 128:(lb + 1) * 128],
                        ).then_inc(sh["xb"], 16)
                        sv["xb"] += 16
            moxb = sv["xb"]

            for lb in range(4):
                lt = c * 4 + lb

                @blk.sync
                def _(sy, lt=lt):
                    sy.wait_ge(sh["sv"], sv["sv"])
                    sy.dma_start(out=xr_sb[:, :],
                                 in_=xw[HALO + lt * 128:HALO + (lt + 1) * 128, :]
                                 ).then_inc(sh["xrd"], 16)
                    sv["xrd"] += 16
                xrc = sv["xrd"]

                def emit_ln(src_ap, moxb_w, sv_w=None):
                    @blk.scalar
                    def _(sc):
                        if moxb_w is not None:
                            sc.wait_ge(sh["xb"], moxb_w)
                        if sv_w is not None:
                            sc.wait_ge(sh["sv"], sv_w)
                        sc.wait_ge(sh["sv"], sv["sv"])
                        inc(sc.activation(out=sq_sb[:, :], in_=src_ap,
                                      func=AF.Square), "sa")
                    sqc = sv["sa"]

                    @blk.vector
                    def _(v, sqc=sqc):
                        v.wait_ge(sh["sa"], sqc)
                        inc(v.reduce_sum(out=st2[:, 0:1], in_=sq_sb[:, :],
                                     axis=mybir.AxisListType.X), "sv")
                        inc(v.reduce_sum(out=st1[:, 0:1], in_=src_ap,
                                     axis=mybir.AxisListType.X), "sv")
                        v.wait_ge(sh["sv"], sv["sv"])
                        inc(v.tensor_scalar_mul(out=st4[:, 0:1], in0=st1[:, 0:1],
                                            scalar1=1.0 / H), "sv")
                        v.wait_ge(sh["sv"], sv["sv"])
                        inc(v.tensor_mul(out=stt[:, 0:1], in0=st4[:, 0:1],
                                     in1=st4[:, 0:1]), "sv")
                        inc(v.tensor_scalar_mul(out=st3[:, 0:1], in0=st2[:, 0:1],
                                            scalar1=1.0 / H), "sv")
                        v.wait_ge(sh["sv"], sv["sv"])
                        inc(v.tensor_sub(out=st3[:, 0:1], in0=st3[:, 0:1],
                                     in1=stt[:, 0:1]), "sv")
                    vc = sv["sv"]

                    @blk.scalar
                    def _(sc, vc=vc):
                        sc.wait_ge(sh["sv"], vc)
                        inc(sc.activation(out=st3[:, 0:1], in_=st3[:, 0:1],
                                      func=AF.Ln, bias=np.float32(EPS)), "sa")
                        sc.wait_ge(sh["sa"], sv["sa"])
                        inc(sc.activation(out=st3[:, 0:1], in_=st3[:, 0:1],
                                      func=AF.Exp, scale=np.float32(-0.5)), "sa")
                    rc = sv["sa"]

                    @blk.vector
                    def _(v, rc=rc):
                        v.wait_ge(sh["sa"], rc)
                        inc(v.tensor_mul(out=st4[:, 0:1], in0=st4[:, 0:1],
                                     in1=st3[:, 0:1]), "sv")
                        v.wait_ge(sh["sv"], sv["sv"])
                        inc(v.tensor_scalar_mul(out=st4[:, 0:1], in0=st4[:, 0:1],
                                            scalar1=-1.0), "sv")
                    return sv["sv"]

                nmc = emit_ln(mo_sb[:, lb, :], moxb)

                @blk.scalar
                def _(sc, lb=lb, nmc=nmc):
                    sc.wait_ge(sh["sv"], nmc)
                    inc(sc.activation(out=sq_sb[:, :], in_=mo_sb[:, lb, :],
                                  func=AF.Identity, scale=st3[:, 0:1], bias=st4[:, 0:1]), "sa")
                r1c = sv["sa"]

                @blk.vector
                def _(v, r1c=r1c, xrc=xrc):
                    v.wait_ge(sh["sa"], r1c)
                    v.wait_ge(sh["xrd"], xrc)
                    inc(v.tensor_add(out=x1_sb[:, :], in0=xr_sb[:, :], in1=sq_sb[:, :]), "sv")
                x1c = sv["sv"]

                @blk.sync
                def _(sy, lt=lt, x1c=x1c):
                    sy.wait_ge(sh["sv"], x1c)
                    sy.dma_start(out=x1_sp[:, lt, :], in_=x1_sb[:, :]
                                 ).then_inc(sh["spl"], 16)
                    sv["spl"] += 16

                nm3 = emit_ln(x1_sb[:, :], None, sv_w=x1c)

                @blk.scalar
                def _(sc, nm3=nm3):
                    sc.wait_ge(sh["sv"], nm3)
                    sc.wait_ge(sh["xb"], sv["xb"])
                    inc(sc.activation(out=xn3_sb[:, :], in_=x1_sb[:, :], func=AF.Identity,
                                  scale=st3[:, 0:1], bias=st4[:, 0:1]), "sa")
                n3c = sv["sa"]

                @blk.sync
                def _(sy, lb=lb, n3c=n3c):
                    sy.wait_ge(sh["sa"], n3c)
                    if lb == 0:
                        sy.wait_ge(sh["spl"], sv["spl"])
                    for ht in range(NHT):
                        sy.dma_start_transpose(
                            x3T_sb[:, ht, lb * 128:(lb + 1) * 128],
                            xn3_sb[:, ht * 128:(ht + 1) * 128],
                        ).then_inc(sh["xb"], 16)
                        sv["xb"] += 16
            x3xb = sv["xb"]

            @blk.sync
            def _(sy, c=c, x3xb=x3xb):
                sy.wait_ge(sh["xb"], x3xb)
                sy.dma_start(out=xn3T_sp[:, :, c * CH:(c + 1) * CH],
                             in_=x3T_sb[:, :, :]).then_inc(sh["spl"], 16)
                sv["spl"] += 16

            if dbg_y is not None:
                @blk.sync
                def _(sy, c=c, ydone=ydone):
                    sy.wait_ge(sh["sv"], ydone)
                    sy.dma_start(out=dbg_y[:, :, c * CH:(c + 1) * CH],
                                 in_=y_sb[:, :, :]).then_inc(sh["spl"], 16)
                    sv["spl"] += 16

            stkC2.close()

        if st_mode:
            final_sv = sv["sv"]

            @blk.gpsimd
            def _(g, final_sv=final_sv):
                g.wait_ge(sh["sv"], final_sv)
                g.dma_start(out=stat_h[:, :], in_=hst_sb[:, :]
                            ).then_inc(sh["od"], 16)
                sv["od"] += 16
                g.dma_start(out=stat_s[:, :], in_=stot_sb[:, :]
                            ).then_inc(sh["od"], 16)
                sv["od"] += 16
                g.wait_ge(sh["od"], sv["od"])
            stkP3.close()
            return nc

        p3_spl = sv["spl"]
        p3_st = sv["st"]
        p3_sa = sv["sa"]
        p3_sv = sv["sv"]

        stkP3.close()
        stkP4 = ExitStack()
        sbt4 = lambda n, s, d: stkP4.enter_context(nc.sbuf_tensor(n, s, d))
        fz_sb = sbt4("fz_sb", [128, NIT, CH], BF16)
        q3_sb = sbt4("q3_sb", [128, NHT, CH], BF16)
        gwt_sb = [sbt4(f"gwt{i}", [128, NHT, 128], BF16) for i in range(2)]
        uwt_sb = [sbt4(f"uwt{i}", [128, NHT, 128], BF16) for i in range(2)]
        dwt_sb = [sbt4(f"dwt{i}", [128, NIT, 128], BF16) for i in range(2)]
        gs_sb = sbt4("gs_sb", [128, CH], BF16)
        us_sb = sbt4("us_sb", [128, CH], BF16)
        foT_sb = [sbt4(f"foT{i}", [128, CH], BF16) for i in range(2)]
        fo_sb = sbt4("fo_sb", [128, 4, H], BF16)
        ot_sb = sbt4("ot_sb", [128, H], F32)

        for q in range(NCHK):
            q0 = q * CH

            @blk.sync
            def _(sy, q=q, q0=q0):
                if q == 0:
                    sy.wait_ge(sh["spl"], p3_spl)
                    sy.wait_ge(sh["st"], p3_st)
                    sy.wait_ge(sh["sa"], p3_sa)
                    sy.wait_ge(sh["sv"], p3_sv)
                    sy.wait_ge(sh["gdf"], gd_all)
                else:
                    sy.wait_ge(sh["st"], sv["st"])
                sy.dma_start(out=q3_sb[:, :, :], in_=xn3T_sp[:, :, q0:q0 + CH]
                             ).then_inc(sh["q3d"], 16)
                sv["q3d"] += 16
            qc = sv["q3d"]

            stF = sv["st"]
            saF = sv["sa"]
            for it in range(NIT):
                b = it % 2

                gun = f"gud{b}"

                @blk.sync
                def _(sy, it=it, b=b, stF=stF, gun=gun):
                    if it >= 2:
                        sy.wait_ge(sh["st"], stF + it - 1)
                    sy.dma_start(
                        out=gwt_sb[b][:, :, :],
                        in_=gwb.rearrange("(ht p) i -> p ht i", p=128)[
                            :, :, it * 128:(it + 1) * 128],
                    ).then_inc(sh[gun], 16)
                    sv[gun] += 16
                    sy.dma_start(
                        out=uwt_sb[b][:, :, :],
                        in_=uwb.rearrange("(ht p) i -> p ht i", p=128)[
                            :, :, it * 128:(it + 1) * 128],
                    ).then_inc(sh[gun], 16)
                    sv[gun] += 16
                gup = sv[gun]

                @blk.tensor
                def _(t, it=it, b=b, gup=gup, qc=qc, saF=saF, gun=gun):
                    t.wait_ge(sh[gun], gup)
                    if it == 0:
                        t.wait_ge(sh["q3d"], qc)
                    if it == 0:
                        t.wait_ge(sh["sa"], saF)
                    else:
                        t.wait_ge(sh["sa"], saF + it)
                    for ht in range(NHT):
                        t.matmul(pD1[:, :], gwt_sb[b][:, ht, :], q3_sb[:, ht, :],
                                 start=(ht == 0), stop=(ht == NHT - 1))
                    for ht in range(NHT):
                        mm = t.matmul(pD2[:, :], uwt_sb[b][:, ht, :],
                                      q3_sb[:, ht, :],
                                      start=(ht == 0), stop=(ht == NHT - 1))
                    inc(mm, "st")
                mmc = sv["st"]

                @blk.scalar
                def _(sc, it=it, mmc=mmc):
                    sc.wait_ge(sh["st"], mmc)
                    sc.wait_ge(sh["sv"], sv["sv"])
                    sc.activation(out=gs_sb[:, :], in_=pD1[:, :], func=AF.Silu,
                                  scale=gsc_sb[:, it:it + 1])
                    inc(sc.activation(out=us_sb[:, :], in_=pD2[:, :], func=AF.Copy,
                                  scale=usc_sb[:, it:it + 1]), "sa")
                evc = sv["sa"]

                @blk.vector
                def _(v, it=it, evc=evc):
                    v.wait_ge(sh["sa"], evc)
                    inc(v.tensor_mul(out=fz_sb[:, it, :], in0=gs_sb[:, :],
                                 in1=us_sb[:, :]), "sv")
            fzc = sv["sv"]

            stD = sv["st"]
            saD = sv["sa"]
            for ht in range(NHT):
                b = ht % 2

                ddn = f"dwd{b}"

                @blk.sync
                def _(sy, ht=ht, b=b, stD=stD, ddn=ddn):
                    if ht >= 2:
                        sy.wait_ge(sh["st"], stD + ht - 1)
                    sy.dma_start(
                        out=dwt_sb[b][:, :, :],
                        in_=dwb.rearrange("(it p) h -> p it h", p=128)[
                            :, :, ht * 128:(ht + 1) * 128],
                    ).then_inc(sh[ddn], 16)
                    sv[ddn] += 16
                dwc = sv[ddn]

                @blk.tensor
                def _(t, ht=ht, b=b, dwc=dwc, fzc=fzc, saD=saD, ddn=ddn):
                    t.wait_ge(sh[ddn], dwc)
                    if ht == 0:
                        t.wait_ge(sh["sv"], fzc)
                        t.wait_ge(sh["sa"], saD)
                    if ht >= 2:
                        t.wait_ge(sh["sa"], saD + ht - 1)
                    bank = pW1 if ht % 2 == 0 else pW2
                    for it in range(NIT):
                        mm = t.matmul(bank[:, :], dwt_sb[b][:, it, :],
                                      fz_sb[:, it, :],
                                      start=(it == 0), stop=(it == NIT - 1))
                    inc(mm, "st")
                mmc = sv["st"]

                @blk.scalar
                def _(sc, ht=ht, mmc=mmc):
                    sc.wait_ge(sh["st"], mmc)
                    if ht < 2:
                        sc.wait_ge(sh["xb"], sv["xb"])
                    bank = pW1 if ht % 2 == 0 else pW2
                    inc(sc.activation(out=foT_sb[ht % 2][:, :], in_=bank[:, :],
                                  func=AF.Copy, scale=dsc_sb[:, ht:ht + 1]), "sa")
                ec = sv["sa"]

                @blk.sync
                def _(sy, ht=ht, ec=ec):
                    sy.wait_ge(sh["sa"], ec)
                    for lb in range(4):
                        sy.dma_start_transpose(
                            fo_sb[:, lb, ht * 128:(ht + 1) * 128],
                            foT_sb[ht % 2][:, lb * 128:(lb + 1) * 128],
                        ).then_inc(sh["xb"], 16)
                        sv["xb"] += 16
            foxb = sv["xb"]

            if dbg_fz is not None:
                @blk.sync
                def _(sy, q=q, q0=q0, fzc=fzc):
                    sy.wait_ge(sh["sv"], fzc)
                    sy.dma_start(out=dbg_fz[:, :, q0:q0 + CH], in_=fz_sb[:, :, :]
                                 ).then_inc(sh["spl"], 16)
                    sv["spl"] += 16

            if dbg_fo is not None:
                @blk.sync
                def _(sy, q=q, foxb=foxb):
                    sy.wait_ge(sh["xb"], foxb)
                    sy.dma_start(out=dbg_fo[:, q * 4:q * 4 + 4, :],
                                 in_=fo_sb[:, :, :]).then_inc(sh["spl"], 16)
                    sv["spl"] += 16

            for lb in range(4):
                lt = q * 4 + lb

                @blk.sync
                def _(sy, lt=lt):
                    sy.wait_ge(sh["sv"], sv["sv"])
                    sy.dma_start(out=xr_sb[:, :], in_=x1_sp[:, lt, :]
                                 ).then_inc(sh["xrd"], 16)
                    sv["xrd"] += 16
                xrc = sv["xrd"]

                @blk.scalar
                def _(sc, lb=lb, foxb=foxb):
                    sc.wait_ge(sh["xb"], foxb)
                    sc.wait_ge(sh["sv"], sv["sv"])
                    inc(sc.activation(out=sq_sb[:, :], in_=fo_sb[:, lb, :],
                                  func=AF.Square), "sa")
                sqc = sv["sa"]

                @blk.vector
                def _(v, lb=lb, sqc=sqc):
                    v.wait_ge(sh["sa"], sqc)
                    inc(v.reduce_sum(out=st2[:, 0:1], in_=sq_sb[:, :],
                                 axis=mybir.AxisListType.X), "sv")
                    inc(v.reduce_sum(out=st1[:, 0:1], in_=fo_sb[:, lb, :],
                                 axis=mybir.AxisListType.X), "sv")
                    v.wait_ge(sh["sv"], sv["sv"])
                    inc(v.tensor_scalar_mul(out=st4[:, 0:1], in0=st1[:, 0:1],
                                        scalar1=1.0 / H), "sv")
                    v.wait_ge(sh["sv"], sv["sv"])
                    inc(v.tensor_mul(out=stt[:, 0:1], in0=st4[:, 0:1], in1=st4[:, 0:1]), "sv")
                    inc(v.tensor_scalar_mul(out=st3[:, 0:1], in0=st2[:, 0:1],
                                        scalar1=1.0 / H), "sv")
                    v.wait_ge(sh["sv"], sv["sv"])
                    inc(v.tensor_sub(out=st3[:, 0:1], in0=st3[:, 0:1], in1=stt[:, 0:1]), "sv")
                vc4 = sv["sv"]

                @blk.scalar
                def _(sc, vc4=vc4):
                    sc.wait_ge(sh["sv"], vc4)
                    inc(sc.activation(out=st3[:, 0:1], in_=st3[:, 0:1], func=AF.Ln,
                                  bias=np.float32(EPS)), "sa")
                    sc.wait_ge(sh["sa"], sv["sa"])
                    inc(sc.activation(out=st3[:, 0:1], in_=st3[:, 0:1], func=AF.Exp,
                                  scale=np.float32(-0.5)), "sa")
                rc4 = sv["sa"]

                @blk.vector
                def _(v, rc4=rc4):
                    v.wait_ge(sh["sa"], rc4)
                    inc(v.tensor_mul(out=st4[:, 0:1], in0=st4[:, 0:1], in1=st3[:, 0:1]), "sv")
                    v.wait_ge(sh["sv"], sv["sv"])
                    inc(v.tensor_scalar_mul(out=st4[:, 0:1], in0=st4[:, 0:1],
                                        scalar1=-1.0), "sv")
                nm4 = sv["sv"]

                @blk.scalar
                def _(sc, lb=lb, nm4=nm4):
                    sc.wait_ge(sh["sv"], nm4)
                    inc(sc.activation(out=sq_sb[:, :], in_=fo_sb[:, lb, :],
                                  func=AF.Identity, scale=st3[:, 0:1], bias=st4[:, 0:1]), "sa")
                r2c = sv["sa"]

                @blk.vector
                def _(v, r2c=r2c, xrc=xrc):
                    v.wait_ge(sh["sa"], r2c)
                    v.wait_ge(sh["xrd"], xrc)
                    v.wait_ge(sh["od"], sv["od"])
                    inc(v.tensor_add(out=ot_sb[:, :], in0=xr_sb[:, :], in1=sq_sb[:, :]), "sv")
                oc = sv["sv"]

                @blk.sync
                def _(sy, lt=lt, oc=oc):
                    sy.wait_ge(sh["sv"], oc)
                    sy.dma_start(out=out[lt * 128:(lt + 1) * 128, :],
                                 in_=ot_sb[:, :]).then_inc(sh["od"], 16)
                    sv["od"] += 16

        @blk.gpsimd
        def _(g):
            g.wait_ge(sh["od"], sv["od"])

        stkP4.close()

    return nc


def _pack_dn(a):
    return np.ascontiguousarray(
        a.reshape(NDT, 128, NST).transpose(1, 0, 2).reshape(128, NDT * NST))


def _pack_d(a):
    n = a.shape[0] // 128
    return np.ascontiguousarray(a.reshape(n, 128).T)


def _selmat():
    import ml_dtypes
    s = np.zeros((32, 2 * NST * 128), np.float32)
    for n in range(NST):
        s[n, n * 128:(n + 1) * 128] = 1.0
        s[NST + n, (NST + n) * 128:(NST + n + 1) * 128] = 1.0
    return s.astype(ml_dtypes.bfloat16)


def _shared_inputs(inp):
    f = np.float32
    return {
        "winT": np.ascontiguousarray(np.asarray(inp["L0_attn_Win"], f).T),
        "wxT": np.ascontiguousarray(np.asarray(inp["L0_attn_Wx"], f).T),
        "wdtT": np.ascontiguousarray(np.asarray(inp["L0_attn_Wdt"], f).T),
        "bdt_pk": _pack_d(np.asarray(inp["L0_attn_bdt"], f)),
        "bconv_pk": _pack_d(np.asarray(inp["L0_attn_bconv"], f)),
        "wconv_pk": np.ascontiguousarray(
            np.asarray(inp["L0_attn_Wconv"], f)[:, 0, :]
            .reshape(NDT, 128, KC).transpose(1, 0, 2)),
        "woutT": np.ascontiguousarray(np.asarray(inp["L0_attn_Wout"], f).T),
        "alog_pk": _pack_dn(np.asarray(inp["L0_attn_A_log"], f)),
        "d_pk": _pack_d(np.asarray(inp["L0_attn_D"], f)),
        "gate_w": np.ascontiguousarray(np.asarray(inp["L0_ffn_gate_w"], f)),
        "up_w": np.ascontiguousarray(np.asarray(inp["L0_ffn_up_w"], f)),
        "down_w": np.ascontiguousarray(np.asarray(inp["L0_ffn_down_w"], f)),
        "gsc_pk": _pack_d(np.asarray(inp["L0_ffn_gate_scale"], f)),
        "usc_pk": _pack_d(np.asarray(inp["L0_ffn_up_scale"], f)),
        "dsc_pk": _pack_d(np.asarray(inp["L0_ffn_down_scale"], f)),
        "selm": _selmat(),
    }


def _xwins(inp):
    f = np.float32
    x = np.ascontiguousarray(np.asarray(inp["L0_attn_x"], f)[0])
    xs = []
    for k in range(NC):
        s = k * OWN
        pad = np.zeros((HALO, H), f) if k == 0 else x[s - HALO:s]
        xs.append(np.ascontiguousarray(np.concatenate([pad, x[s:s + OWN]], 0)))
    return xs


def kernel(**inp):
    f = np.float32
    shared = _shared_inputs(inp)
    xs = _xwins(inp)
    zero_h = np.zeros((128, NC, NDT * NST), f)
    zero_s = np.zeros((128, NC, NDT), f)

    nc1 = build_program("states")
    in1 = [dict(shared, xw=xs[k], prev_h=zero_h, prev_s=zero_s)
           for k in range(NC)]
    r1 = run_bass_kernel_spmd(nc1, in1, core_ids=list(range(NC)))
    hlocs = [np.asarray(r["stat_h"], f) for r in r1.results]
    stots = [np.asarray(r["stat_s"], f) for r in r1.results]

    in2 = []
    for k in range(NC):
        ph = np.zeros((128, NC, NDT * NST), f)
        psv = np.zeros((128, NC, NDT), f)
        for j in range(k):
            ph[:, j, :] = hlocs[j]
            psv[:, j, :] = stots[j]
        in2.append(dict(shared, xw=xs[k], prev_h=ph, prev_s=psv))

    nc2 = build_program("main")
    r2 = run_bass_kernel_spmd(nc2, in2, core_ids=list(range(NC)))
    return np.concatenate([np.asarray(r["out"], f) for r in r2.results],
                          0)[None]
